# version 01
# baseline (speedup 1.0000x reference)
import numpy as np
import contextlib
import concourse.bass as bass
import concourse.mybir as mybir
from concourse.bass_utils import run_bass_kernel_spmd
from concourse.bass_core_rust import IndirectOffsetOnAxis

F32 = mybir.dt.float32
BF16 = mybir.dt.bfloat16
U32 = mybir.dt.uint32
AF = mybir.ActivationFunctionType
ALU = mybir.AluOpType
AX = mybir.AxisListType

ENGINES = ("sync", "gpsimd", "gpsimd_c", "vector", "scalar", "tensor")
DMA_ENGINES = ("sync", "gpsimd")
PHYS = {"sync": "sync", "gpsimd": "gpsimd", "gpsimd_c": "gpsimd",
        "vector": "vector", "scalar": "scalar", "tensor": "tensor"}
PHYS_ENGINES = ("sync", "gpsimd", "vector", "scalar", "tensor")

NCORE = 8
TLOC = 2048
NBLK = TLOC // 128
D = 1024
DC = D // 128


class View:
    __slots__ = ("ap", "space", "arena", "p0", "p1", "b0", "b1", "name")

    def __init__(self, ap, space, arena, p0, p1, b0, b1, name=""):
        self.ap = ap
        self.space = space
        self.arena = arena
        self.p0, self.p1, self.b0, self.b1 = p0, p1, b0, b1
        self.name = name

    def overlaps(self, other):
        if self.space != other.space or self.arena is not other.arena:
            return False
        if self.space == "psum":
            return True
        return not (self.p1 <= other.p0 or other.p1 <= self.p0
                    or self.b1 <= other.b0 or other.b1 <= self.b0)

    def sub(self, ps, pe, es, ee):
        esz = 2 if self.ap.dtype == BF16 else 4
        return View(self.ap[ps:pe, es:ee], self.space, self.arena,
                    self.p0 + ps, self.p0 + pe,
                    self.b0 + es * esz, self.b0 + ee * esz, self.name)

    def col(self, i):
        return self.sub(0, self.p1 - self.p0, i, i + 1)


class Arena:
    def __init__(self, handle, space, words, name):
        self.handle = handle
        self.space = space
        self.words = words
        self.name = name
        self.cursor = 0
        self.records = []


class DramTrack:
    def __init__(self, name):
        self.records = []
        self.name = name
        self.space = "dram"


class Prog:
    def __init__(self, nc: bass.Bass):
        self.nc = nc
        self.streams = {e: [] for e in PHYS_ENGINES}
        self.nops = {e: 0 for e in ENGINES}
        self.dep_targets = {e: set() for e in ENGINES}
        self.waterm = {e: {e2: -1 for e2 in ENGINES} for e in ENGINES}
        self.arenas = []
        self._es = contextlib.ExitStack()
        self.dram_tracks = {}
        self.pubs = {q: {} for q in DMA_ENGINES}

    def sbuf_arena(self, name, words):
        h = self._es.enter_context(self.nc.sbuf_tensor(name, [128, words], F32))
        a = Arena(h, "sbuf", words, name)
        self.arenas.append(a)
        return a

    def psum_bank(self, name):
        h = self._es.enter_context(self.nc.psum_tensor(name, [128, 512], F32))
        a = Arena(h, "psum", 512, name)
        self.arenas.append(a)
        return a

    def view(self, arena, off_words, p0, p1, shape, dtype=F32, name=""):
        p, n = shape
        assert p == p1 - p0, (p, p0, p1)
        ap = arena.handle.ap()
        if dtype == F32:
            assert off_words + n <= arena.words, (arena.name, off_words, n)
            ap = ap[p0:p1, off_words:off_words + n]
            b0, b1 = off_words * 4, off_words * 4 + n * 4
        elif dtype == BF16:
            assert off_words + (n + 1) // 2 <= arena.words
            ap = ap.bitcast(BF16)[p0:p1, 2 * off_words:2 * off_words + n]
            b0, b1 = off_words * 4, off_words * 4 + n * 2
        elif dtype == U32:
            assert off_words + n <= arena.words
            ap = ap.bitcast(U32)[p0:p1, off_words:off_words + n]
            b0, b1 = off_words * 4, off_words * 4 + n * 4
        else:
            raise ValueError(dtype)
        return View(ap, arena.space, arena, p0, p1, b0, b1, name)

    def talloc(self, arena, shape, dtype=F32, name="", align=16):
        p, n = shape
        esz = 2 if dtype == BF16 else 4
        words = (n * esz + 3) // 4
        c = (arena.cursor + align - 1) // align * align
        assert c + words <= arena.words, \
            f"arena {arena.name} overflow: {c}+{words} > {arena.words}"
        arena.cursor = c + words
        return self.view(arena, c, 0, p, shape, dtype, name)

    def dram(self, name, shape, dtype=F32, kind="Internal", addr_space="Local"):
        t = self.nc.dram_tensor(name, list(shape), dtype, kind=kind,
                                addr_space=addr_space)
        tr = DramTrack(name)
        self.dram_tracks[name] = tr
        return DView(t.ap(), tr, name)

    def _get_pub(self, queue, idx):
        pubs = self.pubs[queue]
        cand = [i for i in pubs if i >= idx]
        if cand:
            return min(cand)
        pubs[idx] = None
        return idx

    def _dep(self, eng, reads, writes):
        waits = {}

        def need(peng, pidx):
            if peng == eng and eng == "tensor":
                return
            if pidx <= self.waterm[eng][peng]:
                return
            waits[peng] = max(waits.get(peng, -1), pidx)

        for v in reads + writes:
            for rec in v.arena.records if v.space != "dram" else v.arena.records:
                rv, w, rd = rec
                if not rv.overlaps(v):
                    continue
                if w is not None:
                    need(*w)
        for v in writes:
            for rec in v.arena.records:
                rv, w, rd = rec
                if not rv.overlaps(v):
                    continue
                for peng, pidx in rd.items():
                    need(peng, pidx)
        return waits

    def _commit(self, eng, idx, reads, writes):
        for v in writes:
            recs = v.arena.records
            recs[:] = [r for r in recs if not (
                r[0].overlaps(v)
                and v.p0 <= r[0].p0 and r[0].p1 <= v.p1
                and v.b0 <= r[0].b0 and r[0].b1 <= v.b1)]
            recs.append([v, (eng, idx), {}])
        for v in reads:
            hit = False
            for rec in v.arena.records:
                if rec[0].overlaps(v):
                    rec[2][eng] = max(rec[2].get(eng, -1), idx)
                    hit = True
            if not hit:
                v.arena.records.append([v, None, {eng: idx}])

    def op(self, eng, fn, reads=(), writes=(), tag=""):
        if eng == "gpsimd":
            eng = "gpsimd_c"
        reads = [r for r in reads if r is not None]
        writes = [w for w in writes if w is not None]
        waits = self._dep(eng, reads, writes)
        idx = self.nops[eng]
        stream = self.streams[PHYS[eng]]
        self._emit_waits(eng, stream, waits)
        stream.append(("op", eng, idx, fn, tag))
        self.nops[eng] = idx + 1
        self._commit(eng, idx, reads, writes)
        return idx

    def _emit_waits(self, eng, stream, waits):
        for peng, pidx in sorted(waits.items()):
            if peng in DMA_ENGINES:
                pub = self._get_pub(peng, pidx)
                stream.append(("waitpub", peng, pub))
                self.waterm[eng][peng] = max(self.waterm[eng][peng], pub)
            else:
                self.dep_targets[peng].add(pidx)
                stream.append(("wait", peng, pidx))
                self.waterm[eng][peng] = max(self.waterm[eng][peng], pidx)

    def dma_op(self, queue, fn, reads=(), writes=(), tag=""):
        assert queue in DMA_ENGINES
        reads = [r for r in reads if r is not None]
        writes = [w for w in writes if w is not None]
        waits = self._dep(queue, reads, writes)
        idx = self.nops[queue]
        stream = self.streams[PHYS[queue]]
        self._emit_waits(queue, stream, waits)
        stream.append(("op", queue, idx, fn, tag))
        self.nops[queue] = idx + 1
        self._commit(queue, idx, reads, writes)
        return idx

    def dma(self, queue, out, in_, tag="", **kw):
        def fn(e):
            return e.dma_start(out=out.ap, in_=in_.ap, **kw)
        return self.dma_op(queue, fn, reads=[in_], writes=[out], tag=tag)

    def barrier(self, engines=ENGINES):
        last = {e: self.nops[e] - 1 for e in engines}
        for e in engines:
            for e2 in engines:
                if e2 == e or last[e2] < 0:
                    continue
                if last[e2] <= self.waterm[e][e2]:
                    continue
                if e2 in DMA_ENGINES:
                    pub = self._get_pub(e2, last[e2])
                    self.streams[PHYS[e]].append(("waitpub", e2, pub))
                    self.waterm[e][e2] = max(self.waterm[e][e2], pub)
                else:
                    self.dep_targets[e2].add(last[e2])
                    self.streams[PHYS[e]].append(("wait", e2, last[e2]))
                    self.waterm[e][e2] = last[e2]

    def reset_arena(self, arena):
        arena.cursor = 0
        arena.records.clear()

    def emit(self):
        nc = self.nc
        self.sem_val = {}
        for e in ENGINES:
            if e in DMA_ENGINES:
                continue
            tgts = sorted(self.dep_targets[e])
            self.sem_val[e] = {t: i + 1 for i, t in enumerate(tgts)}
        sems = {e: self._es.enter_context(nc.semaphore(f"sem_{e}"))
                for e in ENGINES}
        self.sems = sems
        block = self._es.enter_context(nc.Block())

        pubrank = {q: {i: r + 1 for r, i in enumerate(sorted(self.pubs[q]))}
                   for q in DMA_ENGINES}
        pubsems = {q: self._es.enter_context(nc.semaphore(f"pub_{q}"))
                   for q in DMA_ENGINES}

        def mk_body(pe):
            def body(builder):
                for item in self.streams[pe]:
                    if item[0] == "wait":
                        _, peng, pidx = item
                        builder.wait_ge(sems[peng], self.sem_val[peng][pidx])
                    elif item[0] == "waitpub":
                        _, q, pidx = item
                        builder.wait_ge(pubsems[q], pubrank[q][pidx])
                    else:
                        _, eng, idx, fn, tag = item
                        inst = fn(builder)
                        if eng in DMA_ENGINES:
                            inst.then_inc(sems[eng], 16)
                            if idx in self.pubs[eng]:
                                builder.wait_ge(sems[eng], (idx + 1) * 16)
                                builder.sem_inc(pubsems[eng], 1)
                        elif idx in self.dep_targets[eng]:
                            inst.then_inc(sems[eng], 1)
            return body

        for pe in PHYS_ENGINES:
            if not self.streams[pe]:
                continue
            getattr(block, pe)(mk_body(pe))
        return block

    def close(self):
        self._es.close()


class DView(View):
    def __init__(self, ap, track, name):
        super().__init__(ap, "dram", track, 0, 1, 0, 1, name)

    def __getitem__(self, key):
        return DView(self.ap[key], self.arena, self.name)

    def rearrange(self, *a, **kw):
        return DView(self.ap.rearrange(*a, **kw), self.arena, self.name)



class Net:

    def __init__(self, p: Prog):
        self.p = p
        self.nc = p.nc
        self.const = p.sbuf_arena("const", 256)
        self.ident = p.talloc(self.const, [128, 128], BF16, "ident")
        self.onesb = p.talloc(self.const, [128, 128], BF16, "onesb")
        self.onesf = p.talloc(self.const, [128, 1], F32, "onesf")
        g = "gpsimd"
        p.op(g, lambda b: b.memset(self.ident.ap, 0.0), writes=[self.ident])
        p.op(g, lambda b: b.affine_select(
            out=self.ident.ap, in_=self.ident.ap, compare_op=ALU.not_equal,
            fill=1.0, base=0, pattern=[[-1, 128]], channel_multiplier=1),
            reads=[self.ident], writes=[self.ident])
        p.op(g, lambda b: b.memset(self.onesb.ap, 1.0), writes=[self.onesb])
        p.op(g, lambda b: b.memset(self.onesf.ap, 1.0), writes=[self.onesf])
        self._consts = {}

    def getc(self, val):
        if val not in self._consts:
            v = self.p.talloc(self.const, [128, 1], F32, f"c{len(self._consts)}",
                              align=1)
            self.p.op("gpsimd", lambda b: b.memset(v.ap, float(val)),
                      writes=[v], tag="const")
            self._consts[val] = v
        return self._consts[val]

    def mm(self, out, lhsT, rhs, start, stop, tag=""):
        def fn(t):
            return t.matmul(out.ap, lhsT.ap, rhs.ap, start=start, stop=stop)
        return self.p.op("tensor", fn, reads=[lhsT, rhs], writes=[out], tag=tag)

    def transp(self, out_psum, in_sbuf, tag=""):
        def fn(t):
            return t.transpose(out_psum.ap, in_sbuf.ap, self.ident.ap)
        return self.p.op("tensor", fn, reads=[in_sbuf, self.ident],
                         writes=[out_psum], tag=tag)

    def v(self, meth, out, ins, *args, tag="", **kw):
        def fn(vb):
            return getattr(vb, meth)(out.ap, *[x.ap for x in ins], *args, **kw)
        return self.p.op("vector", fn, reads=list(ins), writes=[out], tag=tag)

    def vts(self, out, in0, s1, s2, op0, op1=None, tag=""):
        reads = [in0]
        a1 = s1.ap if isinstance(s1, View) else s1
        if isinstance(s1, View):
            reads.append(s1)
        a2 = s2.ap if isinstance(s2, View) else s2
        if isinstance(s2, View):
            reads.append(s2)
        def fn(vb):
            if op1 is None:
                return vb.tensor_single_scalar(out.ap, in0.ap, a1, op0)
            return vb.tensor_scalar(out.ap, in0.ap, a1, a2, op0, op1)
        return self.p.op("vector", fn, reads=reads, writes=[out], tag=tag)

    def vstt(self, out, in0, scal, in1, op0, op1, tag=""):
        reads = [in0, in1]
        a = scal.ap if isinstance(scal, View) else scal
        if isinstance(scal, View):
            reads.append(scal)
        def fn(vb):
            return vb.scalar_tensor_tensor(out.ap, in0.ap, a, in1.ap, op0, op1)
        return self.p.op("vector", fn, reads=reads, writes=[out], tag=tag)

    def act(self, out, in_, func, bias=0.0, scale=1.0, accum=None, tag=""):
        if isinstance(bias, float) and bias != 0.0:
            bias = self.getc(bias)
        if isinstance(bias, View):
            np_ = in_.p1 - in_.p0
            if bias.p1 - bias.p0 != np_:
                bias = bias.sub(0, np_, 0, 1)
        reads = [in_]
        ba = bias.ap if isinstance(bias, View) else bias
        if isinstance(bias, View):
            reads.append(bias)
        sa = scale.ap if isinstance(scale, View) else scale
        if isinstance(scale, View):
            reads.append(scale)
        writes = [out]
        kw = {}
        if accum is not None:
            kw["accum_out"] = accum.ap
            writes.append(accum)
        def fn(s):
            return s.activation(out.ap, in_.ap, func, bias=ba, scale=sa, **kw)
        return self.p.op("scalar", fn, reads=reads, writes=writes, tag=tag)

    def gmemset(self, view, val, tag=""):
        return self.p.op("gpsimd", lambda b: b.memset(view.ap, val),
                         writes=[view], tag=tag)

    def ln_stats_tm(self, x_tile, scratch, eps, n=1024, tag=""):
        p = self.p
        scratch.cursor = 0
        nseg = (n + 511) // 512
        st6 = p.talloc(scratch, [128, 6 * nseg], F32, "st6" + tag)
        mv = p.talloc(scratch, [128, 2], F32, "mv" + tag)
        nm = p.talloc(scratch, [128, 1], F32, "nm" + tag)
        rs = p.talloc(scratch, [128, 1], F32, "rs" + tag)
        for s in range(nseg):
            w = min(512, n - s * 512)
            self.v("bn_stats", st6.sub(0, 128, 6 * s, 6 * s + 6),
                   [x_tile.sub(0, 128, s * 512, s * 512 + w)], tag="bnst" + tag)
        self.v("bn_aggr", mv, [st6], tag="bnag" + tag)
        self.vts(nm, mv.col(0), -1.0, None, ALU.mult, tag="negm" + tag)
        self.act(rs, mv.col(1), AF.Ln, bias=float(eps), scale=1.0, tag="ln" + tag)
        self.act(rs, rs, AF.Exp, bias=0.0, scale=-0.5, tag="exp" + tag)
        return nm, rs


def build_launch_A(nc):
    p = Prog(nc)
    net = Net(p)
    xa = p.dram("xa", [2176, 1024], F32, kind="ExternalInput")
    w_ip = p.dram("w_ip", [1024, 4096], F32, kind="ExternalInput")
    w_cv = p.dram("w_cv", [256, 4 * 2048], F32, kind="ExternalInput")
    cb16 = p.dram("cb16", [128, 16], F32, kind="ExternalInput")
    lg16 = p.dram("lg16", [128, 16], F32, kind="ExternalInput")
    lb16 = p.dram("lb16", [128, 16], F32, kind="ExternalInput")
    w_fc = p.dram("w_fc", [2048, 1024], F32, kind="ExternalInput")
    fb8 = p.dram("fb8", [128, 8], F32, kind="ExternalInput")
    w_cp = p.dram("w_cp", [1024, 2048], F32, kind="ExternalInput")
    pb16 = p.dram("pb16", [128, 16], F32, kind="ExternalInput")
    w_ow = p.dram("w_ow", [2048, 1024], F32, kind="ExternalInput")
    x1 = p.dram("x1", [2048, 1024], F32, kind="ExternalOutput")

    wa = p.sbuf_arena("wa", 24576)
    hb = p.sbuf_arena("hb", 16680)
    wk = p.sbuf_arena("wk", 10200)
    sm = p.sbuf_arena("sm", 512)
    pb = [p.psum_bank(f"pb{i}") for i in range(8)]
    PB = lambda i, n=512, pr=128: p.view(pb[i], 0, 0, pr, [pr, n], F32)

    HOFF = 8

    convb = p.talloc(sm, [128, 16], F32, "convb")
    lng = p.talloc(sm, [128, 16], F32, "lng")
    lnb = p.talloc(sm, [128, 16], F32, "lnb")
    fcb = p.talloc(sm, [128, 8], F32, "fcb")
    cpb = p.talloc(sm, [128, 16], F32, "cpb")
    for dst, src in [(convb, cb16), (lng, lg16), (lnb, lb16), (fcb, fb8),
                     (cpb, pb16)]:
        p.dma("sync", dst, src, tag="smallvec")

    ipT = [p.talloc(wa, [128, 4096], BF16, f"ipT{c}") for c in range(DC)]
    for c in range(DC):
        p.dma("gpsimd", ipT[c], w_ip[c * 128:(c + 1) * 128, :], tag="ldip")
    cvT = [p.talloc(wa, [128, 8192], BF16, f"cvT{i}") for i in range(2)]
    for i in range(2):
        p.dma("gpsimd", cvT[i], w_cv[i * 128:(i + 1) * 128, :], tag="ldcv")

    hbuf = [p.talloc(hb, [128, 2056], BF16, f"hbuf{c}") for c in range(16)]

    xblk = p.talloc(wk, [128, 1024], F32, "xblk")
    lntm = p.talloc(wk, [128, 1024], BF16, "lntm")
    lnfm = [p.talloc(wk, [128, 512], BF16, f"lnfm{c}") for c in range(DC)]
    gate = [p.talloc(wk, [128, 512], BF16, f"gate{c}") for c in range(16)]
    stats = p.sbuf_arena("stats", 512)

    def ln_block_to_fm(xsrc_rows, tw, bo, eps=1e-5):
        p.dma("sync", xblk, xsrc_rows, tag="ldx")
        stats.cursor = 0
        stats.records.clear() if False else None
        nm, rs = net.ln_stats_tm(xblk, stats, eps, 1024)
        net.vts(lntm, xblk, nm, rs, ALU.add, ALU.mult, tag="lnnorm")
        for c in range(DC):
            def fn(e, c=c):
                return e.dma_start(
                    out=lnfm[c].ap[:, bo * 128:bo * 128 + 128],
                    in_=lntm.ap[:, c * 128:(c + 1) * 128], transpose=True)
            p.op("sync", fn, reads=[lntm],
                 writes=[lnfm[c].sub(0, 128, bo * 128, bo * 128 + 128)],
                 tag="lntp")

    def inproj_tile(tw, toff, halo):
        for oc in range(32):
            ps = PB(oc % 2, tw)
            for k in range(DC):
                net.mm(ps, ipT[k].sub(0, 128, oc * 128, (oc + 1) * 128),
                       lnfm[k].sub(0, 128, 0, tw), start=(k == 0),
                       stop=(k == DC - 1), tag="ipmm")
            if halo:
                if oc < 16:
                    net.v("tensor_copy",
                          hbuf[oc].sub(0, 128, HOFF - 3, HOFF),
                          [ps.sub(0, 128, tw - 3, tw)], tag="hevh")
            else:
                if oc < 16:
                    net.v("tensor_copy",
                          hbuf[oc].sub(0, 128, HOFF + toff, HOFF + toff + tw),
                          [ps], tag="hev")
                else:
                    net.act(gate[oc - 16].sub(0, 128, 0, tw), ps, AF.Silu,
                            tag="gev")

    def conv_tile(tw, toff):
        for g in range(8):
            for j in range(2):
                oc = 2 * g + j
                ps = PB(2 + j, tw)
                first = True
                for ic in range(2):
                    hc = g * 2 + ic
                    for k in range(4):
                        lhs = cvT[ic].sub(0, 128, k * 2048 + oc * 128,
                                          k * 2048 + (oc + 1) * 128)
                        rhs = hbuf[hc].sub(0, 128, HOFF + toff + k - 3,
                                           HOFF + toff + k - 3 + tw)
                        net.mm(ps, lhs, rhs, start=first,
                               stop=(ic == 1 and k == 3), tag="cvmm")
                        first = False
            for j in range(2):
                oc = 2 * g + j
                net.vstt(hbuf[oc].sub(0, 128, toff, toff + tw), PB(2 + j, tw),
                         convb.col(oc), gate[oc].sub(0, 128, 0, tw),
                         ALU.add, ALU.mult, tag="cvev")

    ln_block_to_fm(xa[0:128, :], 128, 0)
    inproj_tile(128, -HOFF, halo=True)
    for t in range(4):
        for bo in range(4):
            blk = t * 4 + bo
            ln_block_to_fm(xa[128 + blk * 128:128 + (blk + 1) * 128, :], 128, bo)
        inproj_tile(512, t * 512, halo=False)
        conv_tile(512, t * 512)

    p.barrier()
    p.reset_arena(wa)
    fcT = [p.talloc(wa, [128, 1024], BF16, f"fcT{c}") for c in range(16)]
    for c in range(16):
        p.dma("gpsimd", fcT[c], w_fc[c * 128:(c + 1) * 128, :], tag="ldfc")
    cpT = [p.talloc(wa, [128, 2048], BF16, f"cpT{c}") for c in range(DC)]
    for c in range(DC):
        p.dma("gpsimd", cpT[c], w_cp[c * 128:(c + 1) * 128, :], tag="ldcp")
    owT = [p.talloc(wa, [128, 1024], BF16, f"owT{c}") for c in range(16)]
    for c in range(16):
        p.dma("gpsimd", owT[c], w_ow[c * 128:(c + 1) * 128, :], tag="ldow")

    p.reset_arena(wk)
    TW2 = 256
    hsq = [p.talloc(wk, [128, TW2], BF16, f"hsq{i}") for i in range(2)]
    rows = p.talloc(wk, [1, TW2 * 4], F32, "rows")
    arow_b = p.talloc(wk, [1, TW2], BF16, "arowb")
    brow_b = p.talloc(wk, [1, TW2], BF16, "browb")
    hn = [p.talloc(wk, [128, TW2], BF16, f"hn{c}") for c in range(16)]
    m1 = [p.talloc(wk, [128, TW2], BF16, f"m1{c}") for c in range(DC)]
    h2 = [p.talloc(wk, [128, TW2], BF16, f"h2{c}") for c in range(16)]
    ao = p.talloc(wk, [128, 1024], F32, "ao")
    xblk2 = p.talloc(wk, [128, 1024], F32, "xblk2")
    xnew = p.talloc(wk, [128, 1024], F32, "xnew")

    for t in range(TLOC // TW2):
        toff = t * TW2
        pssum = PB(0, TW2, 1)
        psq = PB(1, TW2, 1)
        for c in range(16):
            hm = hbuf[c].sub(0, 128, toff, toff + TW2)
            net.mm(pssum, net.onesb.sub(0, 128, 0, 1), hm,
                   start=(c == 0), stop=(c == 15), tag="lnsum")
            net.act(hsq[c % 2], hm, AF.Square, tag="sq")
            net.mm(psq, net.onesb.sub(0, 128, 0, 1), hsq[c % 2],
                   start=(c == 0), stop=(c == 15), tag="lnsq")
        mean = rows.sub(0, 1, 0, TW2)
        var = rows.sub(0, 1, TW2, 2 * TW2)
        arow = rows.sub(0, 1, 2 * TW2, 3 * TW2)
        brow = rows.sub(0, 1, 3 * TW2, 4 * TW2)
        net.vts(mean, pssum, 1.0 / 2048, None, ALU.mult, tag="mean")
        net.vts(var, psq, 1.0 / 2048, None, ALU.mult, tag="ex2")
        net.act(arow, mean, AF.Square, tag="m2")
        net.v("tensor_sub", var, [var, arow], tag="var")
        net.act(arow, var, AF.Ln, bias=1e-5, scale=1.0, tag="lnv")
        net.act(arow, arow, AF.Exp, bias=0.0, scale=-0.5, tag="rstd")
        net.v("tensor_mul", brow, [mean, arow], tag="mr")
        net.vts(brow, brow, -1.0, None, ALU.mult, tag="negmr")
        net.v("tensor_copy", arow_b, [arow], tag="castA")
        net.v("tensor_copy", brow_b, [brow], tag="castB")
        psA = PB(2, TW2)
        psB = PB(3, TW2)
        net.mm(psA, net.onesb.sub(0, 1, 0, 128), arow_b, True, True, tag="bcA")
        net.mm(psB, net.onesb.sub(0, 1, 0, 128), brow_b, True, True, tag="bcB")
        for c in range(16):
            hm = hbuf[c].sub(0, 128, toff, toff + TW2)
            net.v("tensor_mul", hn[c], [hm, psA], tag="ap1")
            net.v("tensor_add", hn[c], [hn[c], psB], tag="ap2")
            net.vts(hn[c], hn[c], lng.col(c), lnb.col(c), ALU.mult, ALU.add,
                    tag="ap3")
        for oc in range(DC):
            ps = PB(4 + oc % 2, TW2)
            for k in range(16):
                net.mm(ps, fcT[k].sub(0, 128, oc * 128, (oc + 1) * 128),
                       hn[k], start=(k == 0), stop=(k == 15), tag="fcmm")
            net.act(m1[oc], ps, AF.Silu, bias=fcb.col(oc), tag="fcev")
        for oc in range(16):
            ps = PB(6 + oc % 2, TW2)
            for k in range(DC):
                net.mm(ps, cpT[k].sub(0, 128, oc * 128, (oc + 1) * 128),
                       m1[k], start=(k == 0), stop=(k == DC - 1), tag="cpmm")
            net.vstt(h2[oc], ps, cpb.col(oc),
                     hbuf[oc].sub(0, 128, toff, toff + TW2), ALU.add, ALU.add,
                     tag="cpev")
        for bo in range(TW2 // 128):
            ps0 = PB(0, 512)
            ps1 = PB(1, 512)
            for k in range(16):
                lhs = h2[k].sub(0, 128, bo * 128, (bo + 1) * 128)
                net.mm(ps0, lhs, owT[k].sub(0, 128, 0, 512),
                       start=(k == 0), stop=(k == 15), tag="owmm")
                net.mm(ps1, lhs, owT[k].sub(0, 128, 512, 1024),
                       start=(k == 0), stop=(k == 15), tag="owmm")
            net.v("tensor_copy", ao.sub(0, 128, 0, 512), [ps0], tag="aoev")
            net.v("tensor_copy", ao.sub(0, 128, 512, 1024), [ps1], tag="aoev")
            stats.cursor = 0
            nm, rs = net.ln_stats_tm(ao, stats, 1e-5, 1024, tag="post")
            blk = (toff + bo * 128) // 128
            p.dma("sync", xblk2, xa[128 + blk * 128:128 + (blk + 1) * 128, :],
                  tag="ldx2")
            net.vts(ao, ao, nm, rs, ALU.add, ALU.mult, tag="postn")
            net.v("tensor_add", xnew, [xblk2, ao], tag="resid")
            p.dma("sync", x1[blk * 128:(blk + 1) * 128, :], xnew, tag="stx")

    p.barrier()
    p.emit()
    return p


def build_launch_B(nc):
    p = Prog(nc)
    net = Net(p)
    xb = p.dram("xb", [2048, 1024], F32, kind="ExternalInput")
    gwt = p.dram("gwt", [1024, 32], F32, kind="ExternalInput")
    gbbc = p.dram("gbbc", [128, 32], F32, kind="ExternalInput")
    wg2 = p.dram("wg2", [32 * 1024, 512], F32, kind="ExternalInput")
    wu2 = p.dram("wu2", [32 * 1024, 512], F32, kind="ExternalInput")
    wd2 = p.dram("wd2", [32 * 512, 1024], F32, kind="ExternalInput")
    swg = p.dram("swg", [1024, 512], F32, kind="ExternalInput")
    swu = p.dram("swu", [1024, 512], F32, kind="ExternalInput")
    swd = p.dram("swd", [512, 1024], F32, kind="ExternalInput")
    awg = p.dram("awg", [16 * 1024, 256], F32, kind="ExternalInput")
    awu = p.dram("awu", [16 * 1024, 256], F32, kind="ExternalInput")
    awd = p.dram("awd", [16 * 256, 1024], F32, kind="ExternalInput")
    x2 = p.dram("x2", [2048, 1024], F32, kind="ExternalOutput")
    import os
    DBG = os.environ.get("KERN_DBG") == "1"
    if DBG:
        dbg_cw = p.dram("dbg_cw", [2048, 32], F32, kind="ExternalOutput")
        dbg_aw = p.dram("dbg_aw", [2048, 16], F32, kind="ExternalOutput")
        dbg_acc = p.dram("dbg_acc", [2048, 1024], F32, kind="ExternalOutput")
        dbg_ln = p.dram("dbg_ln", [2048, 1024], F32, kind="ExternalOutput")
        dbg_rt = p.dram("dbg_rt", [128, 256], F32, kind="ExternalOutput")

    lnA = p.sbuf_arena("lnA", 8192 + 64)
    acc = p.sbuf_arena("acc", 16384 + 64)
    wbuf = p.sbuf_arena("wbuf", 13000)
    wk = p.sbuf_arena("wk", 12000)
    stats = p.sbuf_arena("stats", 512)
    pb = [p.psum_bank(f"pb{i}") for i in range(8)]
    PB = lambda i, n=512, pr=128: p.view(pb[i], 0, 0, pr, [pr, n], F32)

    lnfm = [p.talloc(lnA, [128, 2048], BF16, f"lnfm{c}") for c in range(DC)]
    macc = [p.talloc(acc, [128, 1024], F32, f"macc{b}") for b in range(16)]

    gwT = p.talloc(wk, [128, 32 * DC], BF16, "gwT")
    for c in range(DC):
        p.dma("gpsimd", gwT.sub(0, 128, c * 32, (c + 1) * 32),
              gwt[c * 128:(c + 1) * 128, :], tag="ldgw")
    gbb = p.talloc(wk, [128, 32], F32, "gbb")
    p.dma("sync", gbb, gbbc, tag="ldgb")

    xblk = p.talloc(wk, [128, 1024], F32, "xblk")
    lntm = p.talloc(wk, [128, 1024], BF16, "lntm")
    cwb = [p.talloc(wk, [128, 32], F32, f"cw{b}") for b in range(16)]
    awb = [p.talloc(wk, [128, 16], F32, f"aw{b}") for b in range(16)]
    rt = p.talloc(wk, [128, 256], F32, "rt")
    net.gmemset(rt.sub(0, 128, 204, 208), -1e30, tag="gscpad")

    for blk in range(16):
        p.dma("sync", xblk, xb[blk * 128:(blk + 1) * 128, :], tag="ldx")
        stats.cursor = 0
        nm, rs = net.ln_stats_tm(xblk, stats, 1e-5, 1024)
        net.vts(lntm, xblk, nm, rs, ALU.add, ALU.mult, tag="lnnorm")
        for c in range(DC):
            def fn(e, c=c, blk=blk):
                return e.dma_start(
                    out=lnfm[c].ap[:, blk * 128:(blk + 1) * 128],
                    in_=lntm.ap[:, c * 128:(c + 1) * 128], transpose=True)
            p.dma_op("sync", fn, reads=[lntm],
                     writes=[lnfm[c].sub(0, 128, blk * 128, (blk + 1) * 128)],
                     tag="lntp")
        ps = PB(7, 32)
        for k in range(DC):
            net.mm(ps, lnfm[k].sub(0, 128, blk * 128, (blk + 1) * 128),
                   gwT.sub(0, 128, k * 32, (k + 1) * 32),
                   start=(k == 0), stop=(k == DC - 1), tag="lgmm")
        scores = rt.sub(0, 128, 0, 32)
        sc = rt.sub(0, 128, 32, 64)
        masked = rt.sub(0, 128, 64, 96)
        tw = rt.sub(0, 128, 96, 128)
        top1 = rt.sub(0, 128, 128, 160)
        selv = rt.sub(0, 128, 160, 192)
        t8a = rt.sub(0, 128, 192, 200)
        gsc = rt.sub(0, 128, 200, 208)
        g8 = rt.sub(0, 128, 208, 216)
        gm9 = rt.sub(0, 128, 216, 220)
        den = rt.sub(0, 128, 220, 221)
        t8b = rt.sub(0, 128, 224, 232)
        net.act(scores, ps, AF.Sigmoid, tag="sig")
        net.v("tensor_add", sc, [scores, gbb], tag="scb")
        for g in range(4):
            net.v("max", t8a, [sc.sub(0, 128, 8 * g, 8 * g + 8)], tag="gmax")
            net.v("tensor_add", gsc.col(g), [t8a.col(0), t8a.col(1)], tag="gsum")
        net.v("max", g8, [gsc], tag="g2max")
        net.vts(gm9, gsc.sub(0, 128, 0, 4), g8.col(1), None, ALU.is_ge, tag="gmask")
        net.vts(gm9, gm9, 1e9, -1e9, ALU.mult, ALU.add, tag="gm9")
        for g in range(4):
            net.vts(masked.sub(0, 128, 8 * g, 8 * g + 8),
                    sc.sub(0, 128, 8 * g, 8 * g + 8), gm9.col(g), None,
                    ALU.add, tag="mskg")
        net.v("max", t8b, [masked], tag="top8")
        net.vts(selv, masked, t8b.col(5), None, ALU.is_ge, tag="sel")
        net.v("tensor_mul", tw, [selv, scores], tag="tw")
        net.v("reduce_sum", den, [tw], AX.X, tag="den")
        net.vts(den, den, 1e-20, None, ALU.add, tag="deneps")
        net.v("reciprocal", den, [den], tag="rec")
        net.vts(cwb[blk], tw, den.col(0), 0.2, ALU.mult, ALU.mult, tag="cw")
        net.vts(top1, masked, t8b.col(0), None, ALU.is_ge, tag="top1")
        def fn2(vb, blk=blk, top1=top1):
            return vb.tensor_add(awb[blk].ap, top1.ap[:, 0:32:2],
                                 top1.ap[:, 1:32:2])
        p.op("vector", fn2, reads=[top1], writes=[awb[blk]], tag="acw")
        net.vts(awb[blk], awb[blk], 0.1, None, ALU.mult, tag="acw01")
        if DBG and blk == 0:
            p.dma("sync", dbg_rt, rt, tag="dbgrt")

    hsb = [p.talloc(wk, [128, 512], BF16, f"hsb{i}") for i in range(4)]

    def ffn_block(wgv, wuv, wdv, ichunks, scale_cols, first_acc, tag):
        for tt in range(4):
            toff = tt * 512
            for oc in range(ichunks):
                psg = PB(oc % 4, 512)
                psu = PB(4 + oc % 4, 512)
                for k in range(DC):
                    rhs = lnfm[k].sub(0, 128, toff, toff + 512)
                    net.mm(psg, wgv[k].sub(0, 128, oc * 128, (oc + 1) * 128),
                           rhs, start=(k == 0), stop=(k == DC - 1), tag="g" + tag)
                    net.mm(psu, wuv[k].sub(0, 128, oc * 128, (oc + 1) * 128),
                           rhs, start=(k == 0), stop=(k == DC - 1), tag="u" + tag)
                net.act(hsb[oc], psg, AF.Silu, tag="si" + tag)
                net.v("tensor_mul", hsb[oc], [hsb[oc], psu], tag="hu" + tag)
            for bo in range(4):
                blk = tt * 4 + bo
                ps0 = PB(bo % 2 * 2, 512)
                ps1 = PB(bo % 2 * 2 + 1, 512)
                for ic in range(ichunks):
                    lhs = hsb[ic].sub(0, 128, bo * 128, (bo + 1) * 128)
                    net.mm(ps0, lhs, wdv[ic].sub(0, 128, 0, 512),
                           start=(ic == 0), stop=(ic == ichunks - 1),
                           tag="d" + tag)
                    net.mm(ps1, lhs, wdv[ic].sub(0, 128, 512, 1024),
                           start=(ic == 0), stop=(ic == ichunks - 1),
                           tag="d" + tag)
                for half, psx in ((0, ps0), (1, ps1)):
                    av = macc[blk].sub(0, 128, half * 512, half * 512 + 512)
                    if first_acc:
                        net.v("tensor_copy", av, [psx], tag="ic" + tag)
                    elif scale_cols is None:
                        net.v("tensor_add", av, [av, psx], tag="aa" + tag)
                    else:
                        net.vstt(av, psx, scale_cols[blk], av,
                                 ALU.mult, ALU.add, tag="sa" + tag)

    ebuf = []
    for par in range(2):
        gv = [p.talloc(wbuf, [128, 512], BF16, f"eg{par}_{k}") for k in range(DC)]
        uv = [p.talloc(wbuf, [128, 512], BF16, f"eu{par}_{k}") for k in range(DC)]
        dv = [p.talloc(wbuf, [128, 1024], BF16, f"ed{par}_{k}") for k in range(4)]
        ebuf.append((gv, uv, dv))

    def load_gud(par, gsrc, usrc, dsrc, r0, rd0, ni, nd):
        gv, uv, dv = ebuf[par]
        for k in range(DC):
            p.dma("gpsimd", gv[k].sub(0, 128, 0, ni),
                  gsrc[r0 + k * 128:r0 + (k + 1) * 128, :], tag="ldg")
            p.dma("gpsimd", uv[k].sub(0, 128, 0, ni),
                  usrc[r0 + k * 128:r0 + (k + 1) * 128, :], tag="ldu")
        for k in range(nd):
            p.dma("gpsimd", dv[k], dsrc[rd0 + k * 128:rd0 + (k + 1) * 128, :],
                  tag="ldd")

    ecols = [[cwb[b].col(e) for b in range(16)] for e in range(32)]
    acols = [[awb[b].col(l) for b in range(16)] for l in range(16)]

    load_gud(0, swg, swu, swd, 0, 0, 512, 4)
    load_gud(1, wg2, wu2, wd2, 0, 0, 512, 4)
    gv, uv, dv = ebuf[0]
    ffn_block(gv, uv, dv, 4, None, True, "sh")
    for e in range(32):
        par = (e + 1) % 2
        nxt = (e + 2) % 2
        if e + 1 < 32:
            load_gud(nxt, wg2, wu2, wd2, (e + 1) * 1024, (e + 1) * 512, 512, 4)
        elif e + 1 == 32:
            load_gud(nxt, awg, awu, awd, 0, 0, 256, 2)
        gv, uv, dv = ebuf[par]
        ffn_block(gv, uv, dv, 4, ecols[e], False, f"e{e}")
    for l in range(16):
        par = (l + 1) % 2
        nxt = (l + 2) % 2
        if l + 1 < 16:
            load_gud(nxt, awg, awu, awd, (l + 1) * 1024, (l + 1) * 256, 256, 2)
        gv, uv, dv = ebuf[par]
        gv2 = [x.sub(0, 128, 0, 256) for x in gv]
        uv2 = [x.sub(0, 128, 0, 256) for x in uv]
        ffn_block(gv2, uv2, dv, 2, acols[l], False, f"a{l}")

    if DBG:
        lnf32 = p.talloc(wk, [128, 1024], F32, "lnf32")
        for blk in range(16):
            net.v("tensor_copy", lnf32, [lntm]) if False else None
            p.dma("sync", dbg_cw[blk * 128:(blk + 1) * 128, :], cwb[blk])
            p.dma("sync", dbg_aw[blk * 128:(blk + 1) * 128, :], awb[blk])
            p.dma("sync", dbg_acc[blk * 128:(blk + 1) * 128, :], macc[blk])
    xnew = p.talloc(wk, [128, 1024], F32, "xnew")
    for blk in range(16):
        stats.cursor = 0
        nm, rs = net.ln_stats_tm(macc[blk], stats, 1e-5, 1024, tag="post")
        p.dma("sync", xblk, xb[blk * 128:(blk + 1) * 128, :], tag="ldx2")
        net.vts(macc[blk], macc[blk], nm, rs, ALU.add, ALU.mult, tag="postn")
        net.v("tensor_add", xnew, [xblk, macc[blk]], tag="resid")
        p.dma("sync", x2[blk * 128:(blk + 1) * 128, :], xnew, tag="stx")

    p.barrier()
    p.emit()
    return p


def build_launch_C1(nc):
    p = Prog(nc)
    net = Net(p)
    xc = p.dram("xc", [2048, 1024], F32, kind="ExternalInput")
    wdq = p.dram("wdq", [1024, 4096], F32, kind="ExternalInput")
    wuq = p.dram("wuq", [4096, 1024], F32, kind="ExternalInput")
    wdkv = p.dram("wdkv", [1024, 256], F32, kind="ExternalInput")
    wuk = p.dram("wuk", [256, 1024], F32, kind="ExternalInput")
    wuv = p.dram("wuv", [256, 1024], F32, kind="ExternalInput")
    wqr = p.dram("wqr", [4096, 512], F32, kind="ExternalInput")
    wkr = p.dram("wkr", [1024, 32], F32, kind="ExternalInput")
    rope4 = p.dram("rope4", [128, 2048], F32, kind="ExternalInput")
    qkv = p.dram("qkv", [3616, 2048], BF16, kind="ExternalOutput")
    cqd = p.dram("cqd", [4096, 2048], BF16, kind="Internal")

    lnA = p.sbuf_arena("lnA", 8192 + 64)
    wa = p.sbuf_arena("wa", 24600)
    wk = p.sbuf_arena("wk", 12000)
    stats = p.sbuf_arena("stats", 512)
    pb = [p.psum_bank(f"pb{i}") for i in range(8)]
    PB = lambda i, n=512, pr=128: p.view(pb[i], 0, 0, pr, [pr, n], F32)

    lnfm = [p.talloc(lnA, [128, 2048], BF16, f"lnfm{c}") for c in range(DC)]
    xblk = p.talloc(wk, [128, 1024], F32, "xblk")
    lntm = p.talloc(wk, [128, 1024], BF16, "lntm")
    rp4 = p.talloc(wk, [128, 2048], F32, "rp4")
    p.dma("sync", rp4, rope4, tag="ldrope")

    for blk in range(16):
        p.dma("sync", xblk, xc[blk * 128:(blk + 1) * 128, :], tag="ldx")
        stats.cursor = 0
        nm, rs = net.ln_stats_tm(xblk, stats, 1e-5, 1024)
        net.vts(lntm, xblk, nm, rs, ALU.add, ALU.mult, tag="lnnorm")
        for c in range(DC):
            def fn(e, c=c, blk=blk):
                return e.dma_start(
                    out=lnfm[c].ap[:, blk * 128:(blk + 1) * 128],
                    in_=lntm.ap[:, c * 128:(c + 1) * 128], transpose=True)
            p.dma_op("sync", fn, reads=[lntm],
                     writes=[lnfm[c].sub(0, 128, blk * 128, (blk + 1) * 128)],
                     tag="lntp")

    dqT = [p.talloc(wa, [128, 4096], BF16, f"dqT{c}") for c in range(DC)]
    for c in range(DC):
        p.dma("gpsimd", dqT[c], wdq[c * 128:(c + 1) * 128, :], tag="lddq")
    dkvT = [p.talloc(wk, [128, 256], BF16, f"dkvT{c}") for c in range(DC)]
    ukT = [p.talloc(wk, [128, 1024], BF16, f"ukT{c}") for c in range(2)]
    uvT = [p.talloc(wk, [128, 1024], BF16, f"uvT{c}") for c in range(2)]
    krT = [p.talloc(wk, [128, 32], BF16, f"krT{c}") for c in range(DC)]
    for c in range(DC):
        p.dma("gpsimd", dkvT[c], wdkv[c * 128:(c + 1) * 128, :], tag="lddkv")
        p.dma("gpsimd", krT[c], wkr[c * 128:(c + 1) * 128, :], tag="ldkr")
    for c in range(2):
        p.dma("gpsimd", ukT[c], wuk[c * 128:(c + 1) * 128, :], tag="lduk")
        p.dma("gpsimd", uvT[c], wuv[c * 128:(c + 1) * 128, :], tag="lduv")

    ckv = [p.talloc(wk, [128, 2048], BF16, f"ckv{c}") for c in range(2)]
    ev = p.talloc(wk, [128, 512], BF16, "ev")
    ev2 = p.talloc(wk, [128, 512], BF16, "ev2")
    cqt = [p.talloc(lnA, [128, 512], BF16, f"cqt{i}") for i in range(0)]

    for t in range(4):
        toff = t * 512
        for oc in range(2):
            ps = PB(oc % 2, 512)
            for k in range(DC):
                net.mm(ps, dkvT[k].sub(0, 128, oc * 128, (oc + 1) * 128),
                       lnfm[k].sub(0, 128, toff, toff + 512),
                       start=(k == 0), stop=(k == DC - 1), tag="ckvmm")
            net.v("tensor_copy", ckv[oc].sub(0, 128, toff, toff + 512), [ps],
                  tag="ckvev")
        for oc in range(DC):
            psk = PB(2 + oc % 2, 512)
            psv = PB(4 + oc % 2, 512)
            for k in range(2):
                src = ckv[k].sub(0, 128, toff, toff + 512)
                net.mm(psk, ukT[k].sub(0, 128, oc * 128, (oc + 1) * 128), src,
                       start=(k == 0), stop=(k == 1), tag="kcmm")
                net.mm(psv, uvT[k].sub(0, 128, oc * 128, (oc + 1) * 128), src,
                       start=(k == 0), stop=(k == 1), tag="vmm")
            net.v("tensor_copy", ev, [psk], tag="kcev")
            p.dma("sync", p.dram_sub(qkv, 1536 + oc * 128, 128, toff, 512), ev,
                  tag="stkc") if False else None
            p.dma("sync", DView(qkv.ap[1536 + oc * 128:1536 + (oc + 1) * 128,
                                       toff:toff + 512], qkv.arena, "qkv"),
                  ev, tag="stkc")
            net.v("tensor_copy", ev2, [psv], tag="vev")
            p.dma("sync", DView(qkv.ap[2592 + oc * 128:2592 + (oc + 1) * 128,
                                       toff:toff + 512], qkv.arena, "qkv"),
                  ev2, tag="stv")
        psr = PB(6, 512, 32)
        for k in range(DC):
            net.mm(psr, krT[k], lnfm[k].sub(0, 128, toff, toff + 512),
                   start=(k == 0), stop=(k == DC - 1), tag="krmm")
        net.v("tensor_mul", ev.sub(0, 32, 0, 512), [psr,
              rp4.sub(0, 32, toff, toff + 512)], tag="krev")
        p.dma("sync", DView(qkv.ap[2560:2592, toff:toff + 512], qkv.arena,
                            "qkv"), ev.sub(0, 32, 0, 512), tag="stkr")
        for oc in range(32):
            ps = PB(oc % 2, 512)
            for k in range(DC):
                net.mm(ps, dqT[k].sub(0, 128, oc * 128, (oc + 1) * 128),
                       lnfm[k].sub(0, 128, toff, toff + 512),
                       start=(k == 0), stop=(k == DC - 1), tag="cqmm")
            net.v("tensor_copy", ev, [ps], tag="cqev")
            p.dma("sync", DView(cqd.ap[oc * 128:(oc + 1) * 128, toff:toff + 512],
                                cqd.arena, "cqd"), ev, tag="stcq")

    p.barrier()
    p.reset_arena(wa)
    uqT = [p.talloc(wa, [128, 1024], BF16, f"uqT{c}") for c in range(32)]
    qrT = [p.talloc(wa, [128, 512], BF16, f"qrT{c}") for c in range(32)]
    for c in range(32):
        p.dma("gpsimd", uqT[c], wuq[c * 128:(c + 1) * 128, :], tag="lduq")
        p.dma("gpsimd", qrT[c], wqr[c * 128:(c + 1) * 128, :], tag="ldqr")
    p.reset_arena(wk)
    cqs = [p.talloc(wk, [128, 512], BF16, f"cqs{c}") for c in range(32)]
    ev = p.talloc(wk, [128, 512], BF16, "evb")
    rp4b = p.talloc(wk, [128, 2048], F32, "rp4b")
    p.dma("sync", rp4b, rope4, tag="ldrope2")
    for t in range(4):
        toff = t * 512
        for c in range(32):
            p.dma("sync", cqs[c],
                  DView(cqd.ap[c * 128:(c + 1) * 128, toff:toff + 512],
                        cqd.arena, "cqd"), tag="ldcq")
        for oc in range(DC):
            ps = PB(oc % 2, 512)
            for k in range(32):
                net.mm(ps, uqT[k].sub(0, 128, oc * 128, (oc + 1) * 128),
                       cqs[k], start=(k == 0), stop=(k == 31), tag="qcmm")
            net.v("tensor_copy", ev, [ps], tag="qcev")
            p.dma("sync", DView(qkv.ap[oc * 128:(oc + 1) * 128, toff:toff + 512],
                                qkv.arena, "qkv"), ev, tag="stqc")
        for oc in range(4):
            ps = PB(2 + oc % 2, 512)
            for k in range(32):
                net.mm(ps, qrT[k].sub(0, 128, oc * 128, (oc + 1) * 128),
                       cqs[k], start=(k == 0), stop=(k == 31), tag="qrmm")
            net.v("tensor_mul", ev, [ps, rp4b.sub(0, 128, toff, toff + 512)],
                  tag="qrev")
            p.dma("sync", DView(qkv.ap[1024 + oc * 128:1024 + (oc + 1) * 128,
                                       toff:toff + 512], qkv.arena, "qkv"),
                  ev, tag="stqr")

    p.barrier()
    p.emit()
    return p


SC96 = 0.10206207261596577


def build_launch_C2(nc):
    p = Prog(nc)
    net = Net(p)
    qkvh = p.dram("qkvh", [1824, 4096], BF16, kind="ExternalInput")
    y = p.dram("y", [512, 4096], BF16, kind="ExternalOutput")
    import os
    DBG = os.environ.get("KERN_DBG") == "1"
    if DBG:
        dbg_p = p.dram("dbg_p", [128, 4096], BF16, kind="ExternalOutput")
        dbg_pt = p.dram("dbg_pt", [128, 128], BF16, kind="ExternalOutput")
        dbg_vt = p.dram("dbg_vt", [128, 512], BF16, kind="ExternalOutput")

    big = p.sbuf_arena("big", 39000)
    wk = p.sbuf_arena("wk", 6000)
    pb = [p.psum_bank(f"pb{i}") for i in range(8)]
    PB = lambda i, n=512, pr=128: p.view(pb[i], 0, 0, pr, [pr, n], F32)
    pbT = p.view(pb[2], 0, 0, 128, [128, 128], BF16)
    pbY = p.view(pb[3], 0, 0, 64, [64, 128], F32)

    qC = [p.talloc(big, [128, 4096], BF16, f"qC{c}") for c in range(4)]
    qR = [p.talloc(big, [128, 4096], BF16, f"qR{c}") for c in range(4)]
    kC = [p.talloc(big, [128, 4096], BF16, f"kC{c}") for c in range(4)]
    kR2 = p.talloc(big, [128, 4096], BF16, "kR2")
    vT = [p.talloc(big, [128, 512], BF16, f"vT{kb}") for kb in range(32)]
    P = p.talloc(big, [128, 4096], BF16, "P")
    for c in range(4):
        p.dma("sync", qC[c], qkvh[c * 128:(c + 1) * 128, :], tag="ldqc")
        p.dma("sync", kC[c], qkvh[768 + c * 128:768 + (c + 1) * 128, :],
              tag="ldkc")
        p.dma("sync", qR[c].sub(0, 64, 0, 4096),
              qkvh[512 + c * 64:512 + (c + 1) * 64, :], tag="ldqr")
    for a in range(2):
        p.dma("sync", kR2.sub(32 * a, 32 * a + 32, 0, 4096),
              qkvh[1280:1312, :], tag="ldkr")
    for kb in range(32):
        for dc in range(4):
            def fn(e, kb=kb, dc=dc):
                return e.dma_start(
                    out=vT[kb].ap[:, dc * 128:(dc + 1) * 128],
                    in_=qkvh.ap[1312 + dc * 128:1312 + (dc + 1) * 128,
                                kb * 128:(kb + 1) * 128], transpose=True)
            p.dma_op("sync", fn, reads=[qkvh],
                     writes=[vT[kb].sub(0, 128, dc * 128, (dc + 1) * 128)],
                     tag="vtp")

    cmask = p.talloc(wk, [128, 128], F32, "cmask")
    p.op("gpsimd", lambda b: b.memset(cmask.ap, 0.0), writes=[cmask])
    p.op("gpsimd", lambda b: b.affine_select(
        out=cmask.ap, in_=cmask.ap, compare_op=ALU.is_ge, fill=-1e30,
        base=0, pattern=[[-1, 128]], channel_multiplier=1),
        reads=[cmask], writes=[cmask], tag="cmask")
    nmt = p.talloc(wk, [128, 8], F32, "nmt")
    nmsc = p.talloc(wk, [128, 8], F32, "nmsc")
    asum = p.talloc(wk, [128, 8], F32, "asum")
    fr = p.talloc(wk, [128, 8], F32, "fr")
    rmin = p.talloc(wk, [128, 1], F32, "rmin")
    den = p.talloc(wk, [128, 1], F32, "den")
    pTs = [p.talloc(wk, [128, 128], BF16, f"pTs{i}") for i in range(2)]
    ybuf = p.talloc(wk, [64, 128], BF16, "ybuf")

    for h in range(8):
        qc_h = qC[h // 2].sub((h % 2) * 64, (h % 2) * 64 + 64, 0, 4096)
        qr_h = qR[h // 2].sub((h % 2) * 32, (h % 2) * 32 + 32, 0, 4096)
        kc_h = kC[h // 2].sub((h % 2) * 64, (h % 2) * 64 + 64, 0, 4096)
        kr_h = kR2.sub((h % 2) * 32, (h % 2) * 32 + 32, 0, 4096)
        for qb in range(32):
            L = (qb + 1) * 128
            nt = (L + 511) // 512
            q0 = qb * 128
            for j in range(nt):
                w = min(512, L - j * 512)
                psS = PB(j % 2, w)
                net.mm(psS, qc_h.sub(0, 64, q0, q0 + 128),
                       kc_h.sub(0, 64, j * 512, j * 512 + w),
                       start=True, stop=False, tag="smm")
                net.mm(psS, qr_h.sub(0, 32, q0, q0 + 128),
                       kr_h.sub(0, 32, j * 512, j * 512 + w),
                       start=False, stop=True, tag="smm2")
                if j == nt - 1:
                    net.v("tensor_add", psS.sub(0, 128, w - 128, w),
                          [psS.sub(0, 128, w - 128, w), cmask], tag="diagmask")
                net.v("reduce_max", nmt.col(j), [psS], AX.X, negate=True,
                      tag="rmax")
                net.vts(nmsc.col(j), nmt.col(j), SC96, None, ALU.mult,
                        tag="nmsc")
                net.act(P.sub(0, 128, j * 512, j * 512 + w), psS, AF.Exp,
                        bias=nmsc.col(j), scale=SC96,
                        accum=asum.col(j), tag="exp")
            nms = nmsc.sub(0, 128, 0, nt)
            if nt > 1:
                def fnmin(vb, nms=nms):
                    return vb.tensor_reduce(rmin.ap, nms.ap, AX.X, ALU.min)
                p.op("vector", fnmin, reads=[nms], writes=[rmin], tag="rmin")
                net.act(fr.sub(0, 128, 0, nt), nms, AF.Exp, bias=rmin.col(0),
                        scale=-1.0, tag="fj")
                net.v("tensor_mul", nmsc.sub(0, 128, 0, nt),
                      [fr.sub(0, 128, 0, nt), asum.sub(0, 128, 0, nt)],
                      tag="fas")
                net.v("reduce_sum", den, [nmsc.sub(0, 128, 0, nt)], AX.X,
                      tag="densum")
            else:
                net.v("tensor_copy", den, [asum.col(0)], tag="dencp")
                net.gmemset(fr.col(0), 1.0, tag="fr1") if False else \
                    net.v("tensor_copy", fr.col(0), [net.onesf], tag="fr1")
            net.v("reciprocal", den, [den], tag="rden")
            if nt > 1:
                net.vts(fr.sub(0, 128, 0, nt), fr.sub(0, 128, 0, nt),
                        den.col(0), None, ALU.mult, tag="frd")
            else:
                net.v("tensor_copy", fr.col(0), [den], tag="frd1")
            for j in range(nt):
                w = min(512, L - j * 512)
                net.vts(P.sub(0, 128, j * 512, j * 512 + w),
                        P.sub(0, 128, j * 512, j * 512 + w),
                        fr.col(j), None, ALU.mult, tag="pscale")
            nkb = L // 128
            for kb in range(nkb):
                net.transp(pbT, P.sub(0, 128, kb * 128, (kb + 1) * 128),
                           tag="ptp")
                if kb % 2 == 0:
                    net.v("tensor_copy", pTs[0], [pbT], tag="ptev")
                    src = pTs[0]
                else:
                    net.act(pTs[1], pbT, AF.Identity, tag="ptev2")
                    src = pTs[1]
                net.mm(pbY, vT[kb].sub(0, 128, h * 64, (h + 1) * 64), src,
                       start=(kb == 0), stop=(kb == nkb - 1), tag="avmm")
            net.v("tensor_copy", ybuf, [pbY], tag="yev")
            p.dma("sync", DView(y.ap[h * 64:(h + 1) * 64, q0:q0 + 128],
                                y.arena, "y"), ybuf, tag="sty")
            if DBG and h == 0 and qb == 2:
                p.dma("sync", dbg_p, P, tag="dbgp")
                p.dma("sync", dbg_pt, pTs[0], tag="dbgpt")
                p.dma("sync", dbg_vt, vT[0], tag="dbgvt")

    p.barrier()
    p.emit()
    return p


def build_launch_C3(nc):
    p = Prog(nc)
    net = Net(p)
    xc = p.dram("xc3", [2048, 1024], F32, kind="ExternalInput")
    ya = p.dram("ya", [1024, 2048], BF16, kind="ExternalInput")
    wo = p.dram("wo", [1024, 1024], F32, kind="ExternalInput")
    x3 = p.dram("x3", [2048, 1024], F32, kind="ExternalOutput")

    ar = p.sbuf_arena("ar", 16000)
    wk = p.sbuf_arena("wk", 8000)
    stats = p.sbuf_arena("stats", 512)
    pb = [p.psum_bank(f"pb{i}") for i in range(8)]
    PB = lambda i, n=512, pr=128: p.view(pb[i], 0, 0, pr, [pr, n], F32)

    yv = [p.talloc(ar, [128, 2048], BF16, f"yv{c}") for c in range(DC)]
    woT = [p.talloc(ar, [128, 1024], BF16, f"woT{c}") for c in range(DC)]
    for c in range(DC):
        p.dma("sync", yv[c], ya[c * 128:(c + 1) * 128, :], tag="ldy")
        p.dma("gpsimd", woT[c], wo[c * 128:(c + 1) * 128, :], tag="ldwo")
    ao = p.talloc(wk, [128, 1024], F32, "ao")
    xblk = p.talloc(wk, [128, 1024], F32, "xblk")
    xnew = p.talloc(wk, [128, 1024], F32, "xnew")
    for blk in range(16):
        ps0 = PB(blk % 2 * 2, 512)
        ps1 = PB(blk % 2 * 2 + 1, 512)
        for k in range(DC):
            lhs = yv[k].sub(0, 128, blk * 128, (blk + 1) * 128)
            net.mm(ps0, lhs, woT[k].sub(0, 128, 0, 512),
                   start=(k == 0), stop=(k == DC - 1), tag="womm")
            net.mm(ps1, lhs, woT[k].sub(0, 128, 512, 1024),
                   start=(k == 0), stop=(k == DC - 1), tag="womm")
        net.v("tensor_copy", ao.sub(0, 128, 0, 512), [ps0], tag="aoev")
        net.v("tensor_copy", ao.sub(0, 128, 512, 1024), [ps1], tag="aoev")
        stats.cursor = 0
        nm, rs = net.ln_stats_tm(ao, stats, 1e-5, 1024, tag="post")
        p.dma("sync", xblk, xc[blk * 128:(blk + 1) * 128, :], tag="ldx")
        net.vts(ao, ao, nm, rs, ALU.add, ALU.mult, tag="postn")
        net.v("tensor_add", xnew, [xblk, ao], tag="resid")
        p.dma("sync", x3[blk * 128:(blk + 1) * 128, :], xnew, tag="stx")
    p.barrier()
    p.emit()
    return p


def build_launch_D0(nc):
    p = Prog(nc)
    net = Net(p)
    xd = p.dram("xd", [2048, 1024], F32, kind="ExternalInput")
    nwbc = p.dram("nwbc", [128, 1024], F32, kind="ExternalInput")
    nbbc = p.dram("nbbc", [128, 1024], F32, kind="ExternalInput")
    wg = p.dram("wg", [1024, 8], F32, kind="ExternalInput")
    lnd = p.dram("lnd", [2048, 1024], BF16, kind="ExternalOutput")
    ginfo = p.dram("ginfo", [2048, 4], F32, kind="ExternalOutput")
    counts = p.dram("counts", [1, 8], F32, kind="ExternalOutput")

    wk = p.sbuf_arena("wk", 16000)
    stats = p.sbuf_arena("stats", 512)
    pb = [p.psum_bank(f"pb{i}") for i in range(8)]
    PB = lambda i, n=512, pr=128: p.view(pb[i], 0, 0, pr, [pr, n], F32)

    nwt = p.talloc(wk, [128, 1024], F32, "nwt")
    nbt = p.talloc(wk, [128, 1024], F32, "nbt")
    p.dma("sync", nwt, nwbc, tag="ldnw")
    p.dma("sync", nbt, nbbc, tag="ldnb")
    wgT = [p.talloc(wk, [128, 8], BF16, f"wgT{c}") for c in range(DC)]
    for c in range(DC):
        p.dma("gpsimd", wgT[c], wg[c * 128:(c + 1) * 128, :], tag="ldwg")
    ut = p.talloc(wk, [128, 128], BF16, "ut")
    p.op("gpsimd", lambda b: b.memset(ut.ap, 0.0), writes=[ut])
    p.op("gpsimd", lambda b: b.affine_select(
        out=ut.ap, in_=ut.ap, compare_op=ALU.is_ge, fill=1.0,
        base=0, pattern=[[-1, 128]], channel_multiplier=1),
        reads=[ut], writes=[ut], tag="ut")
    io8 = p.talloc(wk, [128, 8], F32, "io8")
    p.op("gpsimd", lambda b: b.iota(io8.ap, pattern=[[1, 8]], base=0,
                                    channel_multiplier=0,
                                    allow_small_or_imprecise_dtypes=True),
         writes=[io8], tag="iota")

    xblk = p.talloc(wk, [128, 1024], F32, "xblk")
    lnx = p.talloc(wk, [128, 1024], F32, "lnx")
    t1 = p.talloc(wk, [128, 1024], F32, "t1")
    lntm = p.talloc(wk, [128, 1024], BF16, "lntm")
    lnfm = [p.talloc(wk, [128, 128], BF16, f"lnfm{c}") for c in range(DC)]
    carry = p.talloc(wk, [1, 8], F32, "carry")
    net.gmemset(carry, 0.0, tag="carry0")
    gi = p.talloc(wk, [128, 4], F32, "gi")
    sc8 = p.talloc(wk, [128, 64], F32, "sc8")
    m1b = p.talloc(wk, [128, 8], BF16, "m1b")
    locf = p.talloc(wk, [128, 8], F32, "locf")
    onesc = p.talloc(wk, [1, 128], BF16, "onesc")
    net.gmemset(onesc, 1.0, tag="onesc")
    onescf = p.talloc(wk, [1, 128], F32, "onescf")
    net.gmemset(onescf, 1.0, tag="onescf")
    carryps = PB(4, 8, 1)
    bcps = PB(5, 8, 128)

    for blk in range(16):
        p.dma("sync", xblk, xd[blk * 128:(blk + 1) * 128, :], tag="ldx")
        stats.cursor = 0
        nm, rs = net.ln_stats_tm(xblk, stats, 1e-5, 1024)
        net.vts(lnx, xblk, nm, rs, ALU.add, ALU.mult, tag="lnx")
        nm2, rs2 = net.ln_stats_tm(lnx, stats, 1e-12, 1024, tag="i")
        net.vts(t1, lnx, nm2, rs2, ALU.add, ALU.mult, tag="lnd1")
        net.v("tensor_mul", t1, [t1, nwt], tag="lnd2")
        net.v("tensor_add", lntm, [t1, nbt], tag="lnd3")
        p.dma("sync", lnd[blk * 128:(blk + 1) * 128, :], lntm, tag="stlnd")
        for c in range(DC):
            def fn(e, c=c):
                return e.dma_start(out=lnfm[c].ap,
                                   in_=lntm.ap[:, c * 128:(c + 1) * 128],
                                   transpose=True)
            p.dma_op("sync", fn, reads=[lntm], writes=[lnfm[c]], tag="lntp")
        ps = PB(blk % 2, 8)
        for k in range(DC):
            net.mm(ps, lnfm[k], wgT[k], start=(k == 0), stop=(k == DC - 1),
                   tag="lgmm")
        logit = sc8.sub(0, 128, 0, 8)
        nmx = sc8.sub(0, 128, 8, 9)
        ex = sc8.sub(0, 128, 9, 17)
        se = sc8.sub(0, 128, 17, 18)
        g1 = sc8.sub(0, 128, 18, 19)
        mask1 = sc8.sub(0, 128, 19, 27)
        idxc = sc8.sub(0, 128, 27, 28)
        locsel = sc8.sub(0, 128, 28, 29)
        tmp8 = sc8.sub(0, 128, 29, 37)
        net.v("tensor_copy", logit, [ps], tag="lgev")
        net.v("reduce_max", nmx, [logit], AX.X, negate=True, tag="lgmax")
        net.act(ex, logit, AF.Exp, bias=nmx.col(0), scale=1.0,
                accum=se, tag="lgexp")
        net.v("reciprocal", g1, [se], tag="g1")
        net.vts(mask1, logit, nmx.col(0), None, ALU.add, tag="lgdiff")
        net.vts(mask1, mask1, 0.0, None, ALU.is_ge, tag="mask1")
        net.v("tensor_mul", tmp8, [mask1, io8], tag="idxm")
        net.v("reduce_sum", idxc, [tmp8], AX.X, tag="idx")
        net.v("tensor_copy", m1b, [mask1], tag="m1b")
        psl = PB(2 + blk % 2, 8)
        net.mm(psl, ut, m1b, start=True, stop=True, tag="cum")
        net.v("tensor_copy", locf, [psl], tag="locev")
        def fbc(t):
            return t.matmul(bcps.ap, onescf.ap, carry.ap, start=True, stop=True)
        p.op("tensor", fbc, reads=[onescf, carry], writes=[bcps], tag="cbc")
        net.v("tensor_add", locf, [locf, bcps], tag="locc")
        net.v("tensor_mul", locf, [locf, mask1], tag="locm")
        net.v("reduce_sum", locsel, [locf], AX.X, tag="locsel")
        net.mm(carryps, net.onesb.sub(0, 128, 0, 1), m1b, True, True,
               tag="csum")
        net.v("tensor_add", carry, [carry, carryps], tag="cupd")
        net.v("tensor_copy", gi.col(0), [idxc], tag="gi0")
        net.v("tensor_copy", gi.col(1), [locsel], tag="gi1")
        net.v("tensor_copy", gi.col(2), [g1], tag="gi2")
        net.v("tensor_copy", gi.col(3), [se], tag="gi3")
        p.dma("sync", ginfo[blk * 128:(blk + 1) * 128, :], gi, tag="stgi")
    p.dma("sync", counts, carry, tag="stcnt")
    p.barrier()
    p.emit()
    return p


def build_launch_D1(nc):
    p = Prog(nc)
    net = Net(p)
    lndf = p.dram("lndf", [16384, 1024], BF16, kind="ExternalInput")
    gall = p.dram("gall", [16384, 4], F32, kind="ExternalInput")
    cnts = p.dram("cnts", [8, 8], F32, kind="ExternalInput")
    pmask = p.dram("pmask", [8, 1], F32, kind="ExternalInput")
    eohbc = p.dram("eohbc", [128, 8], F32, kind="ExternalInput")
    iw = p.dram("iw", [1024, 4096], F32, kind="ExternalInput")
    ib32 = p.dram("ib32", [128, 32], F32, kind="ExternalInput")
    ow = p.dram("ow", [4096, 1024], F32, kind="ExternalInput")
    obbc = p.dram("obbc", [128, 1024], F32, kind="ExternalInput")
    eout = p.dram("eout", [2048, 1024], BF16, kind="ExternalOutput")
    disp = p.dram("disp", [2049, 1024], BF16, kind="Internal")

    wa = p.sbuf_arena("wa", 32800)
    wk = p.sbuf_arena("wk", 14000)
    pb = [p.psum_bank(f"pb{i}") for i in range(8)]
    PB = lambda i, n=512, pr=128: p.view(pb[i], 0, 0, pr, [pr, n], F32)

    cnt = p.talloc(wk, [8, 8], F32, "cnt")
    eoh = p.talloc(wk, [128, 8], F32, "eoh")
    io8 = p.talloc(wk, [128, 8], F32, "io8")
    p.dma("sync", cnt, cnts, tag="ldcnt")
    p.dma("sync", eoh, eohbc, tag="ldeoh")
    p.op("gpsimd", lambda b: b.iota(io8.ap, pattern=[[1, 8]], base=0,
                                    channel_multiplier=0,
                                    allow_small_or_imprecise_dtypes=True),
         writes=[io8], tag="iota")
    sl8 = p.talloc(wk, [8, 8], F32, "sl8")
    id8 = p.talloc(wk, [8, 8], F32, "id8")
    p.op("gpsimd", lambda b: b.memset(sl8.ap, 0.0), writes=[sl8])
    p.op("gpsimd", lambda b: b.affine_select(
        out=sl8.ap, in_=sl8.ap, compare_op=ALU.is_ge, fill=1.0,
        base=0, pattern=[[-1, 8]], channel_multiplier=1),
        reads=[sl8], writes=[sl8], tag="sl8")
    p.op("gpsimd", lambda b: b.memset(id8.ap, 0.0), writes=[id8])
    p.op("gpsimd", lambda b: b.affine_select(
        out=id8.ap, in_=id8.ap, compare_op=ALU.not_equal, fill=1.0,
        base=0, pattern=[[-1, 8]], channel_multiplier=1),
        reads=[id8], writes=[id8], tag="id8")
    offps = PB(6, 8, 8)
    net.mm(offps, sl8, cnt, True, True, tag="offs")
    offsb = p.talloc(wk, [8, 8], F32, "offsb")
    net.v("tensor_copy", offsb, [offps], tag="offev")
    onescf = p.talloc(wk, [1, 128], F32, "onescf")
    net.gmemset(onescf, 1.0, tag="onescf")
    offrows = p.talloc(wk, [1, 64], F32, "offrows")
    for k in range(8):
        pr = PB(6, 8, 1)
        net.mm(pr, id8.sub(0, 8, k, k + 1), offsb, True, True, tag="offrk")
        net.v("tensor_copy", offrows.sub(0, 1, k * 8, k * 8 + 8), [pr],
              tag="offrkev")
    offbc = p.talloc(wk, [128, 64], F32, "offbc")
    for k in range(8):
        pbk = PB(7, 8, 128)
        def fbc(t, k=k):
            return t.matmul(pbk.ap, onescf.ap,
                            offrows.ap[0:1, k * 8:k * 8 + 8],
                            start=True, stop=True)
        p.op("tensor", fbc, reads=[onescf, offrows], writes=[pbk],
             tag="offbc")
        net.v("tensor_copy", offbc.sub(0, 128, k * 8, k * 8 + 8), [pbk],
              tag="offbcev")

    zt = p.talloc(wk, [128, 1024], BF16, "zt")
    net.gmemset(zt, 0.0, tag="zt")
    for b in range(16):
        p.dma("sync", DView(disp.ap[b * 128:(b + 1) * 128, :], disp.arena,
                            "disp"), zt, tag="zdisp")

    gi = p.talloc(wk, [128, 4], F32, "gi")
    lblk = p.talloc(wk, [128, 1024], BF16, "lblk")
    s8 = p.talloc(wk, [128, 24], F32, "s8")
    slotu = p.talloc(wk, [128, 1], U32, "slotu")
    for blk in range(128):
        p.dma("sync", gi, gall[blk * 128:(blk + 1) * 128, :], tag="ldgi")
        p.dma("sync", lblk, lndf[blk * 128:(blk + 1) * 128, :], tag="ldln")
        mask1 = s8.sub(0, 128, 0, 8)
        t8 = s8.sub(0, 128, 8, 16)
        selo = s8.sub(0, 128, 16, 17)
        mine = s8.sub(0, 128, 17, 18)
        locg = s8.sub(0, 128, 18, 19)
        keep = s8.sub(0, 128, 19, 20)
        tt = s8.sub(0, 128, 20, 21)
        slotf = s8.sub(0, 128, 21, 22)
        net.vts(mask1, io8, gi.col(0), None, ALU.is_equal, tag="m1")
        oc_ = blk // 16
        net.v("tensor_mul", t8,
              [mask1, offbc.sub(0, 128, oc_ * 8, oc_ * 8 + 8)], tag="mo")
        net.v("reduce_sum", selo, [t8], AX.X, tag="selo")
        net.v("tensor_mul", t8, [mask1, eoh], tag="me")
        net.v("reduce_sum", mine, [t8], AX.X, tag="mine")
        net.v("tensor_add", locg, [gi.col(1), selo], tag="locg")
        net.vts(keep, locg, 2048.0, None, ALU.is_lt, tag="keep")
        net.v("tensor_mul", tt, [mine, keep], tag="minekeep")
        net.vts(tt, tt, -1e6, 1e6, ALU.mult, ALU.add, tag="oob")
        net.v("tensor_add", slotf, [locg, tt], tag="slotf")
        net.vts(slotf, slotf, 2048.0, None, ALU.min, tag="slotcl")
        net.v("tensor_copy", slotu, [slotf], tag="slotu")
        def fsc(g, slotu=slotu, lblk=lblk):
            return g.indirect_dma_start(
                out=disp.ap, out_offset=IndirectOffsetOnAxis(slotu.ap, 0),
                in_=lblk.ap, in_offset=None)
        p.dma_op("gpsimd", fsc, reads=[slotu, lblk],
                 writes=[DView(disp.ap, disp.arena, "disp")], tag="scat")

    iwT = [p.talloc(wa, [128, 4096], BF16, f"iwT{c}") for c in range(DC)]
    owT = [p.talloc(wa, [128, 1024], BF16, f"owT{c}") for c in range(32)]
    for c in range(DC):
        p.dma("gpsimd", iwT[c], iw[c * 128:(c + 1) * 128, :], tag="ldiw")
    for c in range(32):
        p.dma("gpsimd", owT[c], ow[c * 128:(c + 1) * 128, :], tag="ldow")
    ibt = p.talloc(wk, [128, 32], F32, "ibt")
    obt = p.talloc(wk, [128, 1024], F32, "obt")
    p.dma("sync", ibt, ib32, tag="ldib")
    p.dma("sync", obt, obbc, tag="ldob")
    dfm = [p.talloc(wk, [128, 512], BF16, f"dfm{c}") for c in range(DC)]
    hfm = [p.talloc(wk, [128, 512], BF16, f"hfm{c}") for c in range(32)]
    eo = p.talloc(wk, [128, 1024], BF16, "eo")
    for t in range(4):
        toff = t * 512
        for c in range(DC):
            for bo in range(4):
                tb = t * 4 + bo
                def ftp(e, c=c, bo=bo, tb=tb):
                    return e.dma_start(
                        out=dfm[c].ap[:, bo * 128:(bo + 1) * 128],
                        in_=disp.ap[tb * 128:(tb + 1) * 128,
                                    c * 128:(c + 1) * 128],
                        transpose=True)
                p.dma_op("sync", ftp,
                         reads=[DView(disp.ap, disp.arena, "disp")],
                         writes=[dfm[c].sub(0, 128, bo * 128, (bo + 1) * 128)],
                         tag="dtp")
        for oc in range(32):
            ps = PB(oc % 4, 512)
            for k in range(DC):
                net.mm(ps, iwT[k].sub(0, 128, oc * 128, (oc + 1) * 128),
                       dfm[k].sub(0, 128, 0, 512),
                       start=(k == 0), stop=(k == DC - 1), tag="imm")
            net.act(hfm[oc], ps, AF.Gelu_apprx_tanh, bias=ibt.col(oc),
                    tag="gelu")
        for bo in range(4):
            ps0 = PB(4 + bo % 2 * 2, 512)
            ps1 = PB(4 + bo % 2 * 2 + 1, 512)
            for ic in range(32):
                lhs = hfm[ic].sub(0, 128, bo * 128, (bo + 1) * 128)
                net.mm(ps0, lhs, owT[ic].sub(0, 128, 0, 512),
                       start=(ic == 0), stop=(ic == 31), tag="omm")
                net.mm(ps1, lhs, owT[ic].sub(0, 128, 512, 1024),
                       start=(ic == 0), stop=(ic == 31), tag="omm")
            net.v("tensor_add", eo.sub(0, 128, 0, 512),
                  [ps0, obt.sub(0, 128, 0, 512)], tag="oev")
            net.v("tensor_add", eo.sub(0, 128, 512, 1024),
                  [ps1, obt.sub(0, 128, 512, 1024)], tag="oev")
            blk = t * 4 + bo
            p.dma("sync", eout[blk * 128:(blk + 1) * 128, :], eo, tag="steo")
    p.barrier()
    p.emit()
    return p


def build_launch_E(nc):
    p = Prog(nc)
    net = Net(p)
    xe = p.dram("xe", [2048, 1024], F32, kind="ExternalInput")
    lnde = p.dram("lnde", [2048, 1024], BF16, kind="ExternalInput")
    gie = p.dram("gie", [2048, 4], F32, kind="ExternalInput")
    cnts = p.dram("cnts", [8, 8], F32, kind="ExternalInput")
    pmask = p.dram("pmask", [8, 1], F32, kind="ExternalInput")
    eall = p.dram("eall", [16385, 1024], BF16, kind="ExternalInput")
    riw = p.dram("riw", [1024, 4096], F32, kind="ExternalInput")
    rib32 = p.dram("rib32", [128, 32], F32, kind="ExternalInput")
    rowt = p.dram("rowt", [4096, 1024], F32, kind="ExternalInput")
    robc = p.dram("robc", [128, 1024], F32, kind="ExternalInput")
    rcoef = p.dram("rcoef", [1024, 2], F32, kind="ExternalInput")
    x4 = p.dram("x4", [2048, 1024], F32, kind="ExternalOutput")

    wa = p.sbuf_arena("wa", 33000)
    wk = p.sbuf_arena("wk", 17200)
    stats = p.sbuf_arena("stats", 512)
    pb = [p.psum_bank(f"pb{i}") for i in range(8)]
    PB = lambda i, n=512, pr=128: p.view(pb[i], 0, 0, pr, [pr, n], F32)

    riwT = [p.talloc(wa, [128, 4096], BF16, f"riwT{c}") for c in range(DC)]
    rowT = [p.talloc(wa, [128, 1024], BF16, f"rowT{c}") for c in range(32)]
    for c in range(DC):
        p.dma("gpsimd", riwT[c], riw[c * 128:(c + 1) * 128, :], tag="ldriw")
    for c in range(32):
        p.dma("gpsimd", rowT[c], rowt[c * 128:(c + 1) * 128, :], tag="ldrow")
    rcT = [p.talloc(wk, [128, 2], BF16, f"rcT{c}") for c in range(DC)]
    for c in range(DC):
        p.dma("gpsimd", rcT[c], rcoef[c * 128:(c + 1) * 128, :], tag="ldrc")
    ribt = p.talloc(wk, [128, 32], F32, "ribt")
    robt = p.talloc(wk, [128, 1024], F32, "robt")
    p.dma("sync", ribt, rib32, tag="ldrib")
    p.dma("sync", robt, robc, tag="ldrob")
    cnt = p.talloc(wk, [8, 8], F32, "cnt")
    pm = p.talloc(wk, [8, 1], F32, "pm")
    io8 = p.talloc(wk, [128, 8], F32, "io8")
    p.dma("sync", cnt, cnts, tag="ldcnt")
    p.dma("sync", pm, pmask, tag="ldpm")
    p.op("gpsimd", lambda b: b.iota(io8.ap, pattern=[[1, 8]], base=0,
                                    channel_multiplier=0,
                                    allow_small_or_imprecise_dtypes=True),
         writes=[io8], tag="iota")
    offps = PB(6, 8, 1)
    net.mm(offps, pm, cnt, True, True, tag="offs")
    offrow = p.talloc(wk, [1, 8], F32, "offrow")
    net.v("tensor_copy", offrow, [offps], tag="offev")
    onescf = p.talloc(wk, [1, 128], F32, "onescf")
    net.gmemset(onescf, 1.0, tag="onescf")
    offbps = PB(7, 8, 128)
    def fbc(t):
        return t.matmul(offbps.ap, onescf.ap, offrow.ap, start=True, stop=True)
    p.op("tensor", fbc, reads=[onescf, offrow], writes=[offbps], tag="offbc")
    offbc = p.talloc(wk, [128, 8], F32, "offbc")
    net.v("tensor_copy", offbc, [offbps], tag="offbcev")

    lnfm = [p.talloc(wk, [128, 512], BF16, f"lnfm{c}") for c in range(DC)]

    rhfm = [p.talloc(wk, [128, 512], BF16, f"rhfm{c}") for c in range(32)]
    gi = p.talloc(wk, [128, 4], F32, "gi")
    s8 = p.talloc(wk, [128, 24], F32, "s8")
    gidxu = p.talloc(wk, [128, 1], U32, "gidxu")
    moe = p.talloc(wk, [128, 1024], BF16, "moe")
    rmlp = p.talloc(wk, [128, 1024], F32, "rmlp")
    xblk = p.talloc(wk, [128, 1024], F32, "xblk")
    lnx = p.talloc(wk, [128, 1024], F32, "lnx")
    acc = p.talloc(wk, [128, 1024], F32, "acc")
    cf = p.talloc(wk, [128, 4], F32, "cf")

    for t in range(4):
        toff = t * 512
        for c in range(DC):
            for bo in range(4):
                tb = t * 4 + bo
                def ftp(e, c=c, bo=bo, tb=tb):
                    return e.dma_start(
                        out=lnfm[c].ap[:, bo * 128:(bo + 1) * 128],
                        in_=lnde.ap[tb * 128:(tb + 1) * 128,
                                    c * 128:(c + 1) * 128],
                        transpose=True)
                p.dma_op("sync", ftp, reads=[lnde],
                         writes=[lnfm[c].sub(0, 128, bo * 128,
                                             (bo + 1) * 128)], tag="lntp")
        for oc in range(32):
            ps = PB(oc % 4, 512)
            for k in range(DC):
                net.mm(ps, riwT[k].sub(0, 128, oc * 128, (oc + 1) * 128),
                       lnfm[k].sub(0, 128, 0, 512),
                       start=(k == 0), stop=(k == DC - 1), tag="rimm")
            net.act(rhfm[oc], ps, AF.Gelu_apprx_tanh, bias=ribt.col(oc),
                    tag="rgelu")
        for bo in range(4):
            blk = t * 4 + bo
            ps0 = PB(4, 512)
            ps1 = PB(5, 512)
            for ic in range(32):
                lhs = rhfm[ic].sub(0, 128, bo * 128, (bo + 1) * 128)
                net.mm(ps0, lhs, rowT[ic].sub(0, 128, 0, 512),
                       start=(ic == 0), stop=(ic == 31), tag="romm")
                net.mm(ps1, lhs, rowT[ic].sub(0, 128, 512, 1024),
                       start=(ic == 0), stop=(ic == 31), tag="romm")
            net.v("tensor_add", rmlp.sub(0, 128, 0, 512),
                  [ps0, robt.sub(0, 128, 0, 512)], tag="roev")
            net.v("tensor_add", rmlp.sub(0, 128, 512, 1024),
                  [ps1, robt.sub(0, 128, 512, 1024)], tag="roev")
            psc = PB(6, 2)
            for k in range(DC):
                net.mm(psc, lnfm[k].sub(0, 128, bo * 128, (bo + 1) * 128),
                       rcT[k], start=(k == 0), stop=(k == DC - 1), tag="cfmm")
            net.v("tensor_copy", cf.sub(0, 128, 0, 2), [psc], tag="cfev")
            net.v("tensor_sub", cf.col(2), [cf.col(0), cf.col(1)], tag="cfd")
            net.act(cf.col(2), cf.col(2), AF.Sigmoid, tag="cfs")
            net.vts(cf.col(3), cf.col(2), -1.0, 1.0, ALU.mult, ALU.add,
                    tag="c1")
            p.dma("sync", gi, gie[blk * 128:(blk + 1) * 128, :], tag="ldgi")
            mask1 = s8.sub(0, 128, 0, 8)
            t8 = s8.sub(0, 128, 8, 16)
            selo = s8.sub(0, 128, 16, 17)
            locg = s8.sub(0, 128, 17, 18)
            keep = s8.sub(0, 128, 18, 19)
            gidxf = s8.sub(0, 128, 19, 20)
            gm = s8.sub(0, 128, 20, 21)
            net.vts(mask1, io8, gi.col(0), None, ALU.is_equal, tag="m1")
            net.v("tensor_mul", t8, [mask1, offbc], tag="mo")
            net.v("reduce_sum", selo, [t8], AX.X, tag="selo")
            net.v("tensor_add", locg, [gi.col(1), selo], tag="locg")
            net.vts(keep, locg, 2048.0, None, ALU.is_lt, tag="keep")
            net.vts(gidxf, gi.col(0), 2048.0, None, ALU.mult, tag="gidx1")
            net.v("tensor_add", gidxf, [gidxf, locg], tag="gidx2")
            net.vts(keep, keep, -1e6, 1e6, ALU.mult, ALU.add, tag="gidx3")
            net.v("tensor_add", gidxf, [gidxf, keep], tag="gidx4")
            net.vts(gidxf, gidxf, 16384.0, None, ALU.min, tag="gidxcl")
            net.v("tensor_copy", gidxu, [gidxf], tag="gidxu")
            def fga(g, gidxu=gidxu, moe=moe):
                return g.indirect_dma_start(
                    out=moe.ap, out_offset=None,
                    in_=eall.ap, in_offset=IndirectOffsetOnAxis(gidxu.ap, 0))
            p.dma_op("gpsimd", fga, reads=[eall, gidxu], writes=[moe],
                     tag="gath")
            p.dma("sync", xblk, xe[blk * 128:(blk + 1) * 128, :], tag="ldx")
            stats.cursor = 0
            nm, rs = net.ln_stats_tm(xblk, stats, 1e-5, 1024)
            net.vts(lnx, xblk, nm, rs, ALU.add, ALU.mult, tag="lnx")
            net.v("tensor_mul", gm, [cf.col(3), gi.col(2)], tag="gm")
            net.vts(acc, rmlp, cf.col(2), None, ALU.mult, tag="cmb1")
            net.vstt(acc, moe, gm, acc, ALU.mult, ALU.add, tag="cmb2")
            net.v("tensor_add", acc, [acc, lnx], tag="cmb3")
            nm2, rs2 = net.ln_stats_tm(acc, stats, 1e-5, 1024, tag="f")
            net.vts(acc, acc, nm2, rs2, ALU.add, ALU.mult, tag="fln")
            net.v("tensor_add", acc, [acc, xblk], tag="fres")
            p.dma("sync", x4[blk * 128:(blk + 1) * 128, :], acc, tag="stx")
    p.barrier()
    p.emit()
    return p



def _pm(c):
    pmv = np.zeros((8, 1), np.float32)
    pmv[:c] = 1
    return pmv


def _rope_scale(T=4096, dim=32):
    t = np.arange(T, dtype=np.float32)
    inv = (1.0 / 10000.0 ** (np.arange(0, dim, 2, dtype=np.float32) / dim)).astype(np.float32)
    freqs = t[:, None] * inv[None, :]
    emb = np.concatenate([freqs, freqs], -1)
    return (np.cos(emb) + np.sin(emb)).astype(np.float32)


def kernel(**inp):
    x = np.ascontiguousarray(inp["L0_attn_hidden_states"].reshape(16384, 1024))
    f32 = np.float32
    asf = lambda a: np.ascontiguousarray(np.asarray(a, dtype=f32))

    ncA = bass.Bass("TRN2", target_bir_lowering=False)
    build_launch_A(ncA)
    sharedA = {
        "w_ip": asf(inp["L0_attn_in_proj_w"]).T.copy(),
        "w_cv": asf(inp["L0_attn_conv_w"]).transpose(1, 2, 0).reshape(256, 8192).copy(),
        "cb16": asf(inp["L0_attn_conv_b"]).reshape(16, 128).T.copy(),
        "lg16": asf(inp["L0_attn_ln_g"]).reshape(16, 128).T.copy(),
        "lb16": asf(inp["L0_attn_ln_b"]).reshape(16, 128).T.copy(),
        "w_fc": asf(inp["L0_attn_fc_w"]).T.copy(),
        "fb8": asf(inp["L0_attn_fc_b"]).reshape(8, 128).T.copy(),
        "w_cp": asf(inp["L0_attn_cproj_w"]).T.copy(),
        "pb16": asf(inp["L0_attn_cproj_b"]).reshape(16, 128).T.copy(),
        "w_ow": asf(inp["L0_attn_out_w"]).T.copy(),
    }
    inA = []
    for c in range(8):
        t0 = c * 2048
        halo = np.zeros((128, 1024), f32) if c % 2 == 0 else x[t0 - 128:t0]
        inA.append({"xa": np.concatenate([halo, x[t0:t0 + 2048]], 0), **sharedA})
    rA = run_bass_kernel_spmd(ncA, inA, core_ids=list(range(8)))
    x1 = [rA.results[c]["x1"] for c in range(8)]

    ncB = bass.Bass("TRN2", target_bir_lowering=False)
    build_launch_B(ncB)
    sharedB = {
        "gwt": asf(inp["L0_ffn_gate_weight"]).T.copy(),
        "gbbc": np.broadcast_to(asf(inp["L0_ffn_gate_bias"]), (128, 32)).copy(),
        "wg2": asf(inp["L0_ffn_w_gate"]).reshape(32 * 1024, 512).copy(),
        "wu2": asf(inp["L0_ffn_w_up"]).reshape(32 * 1024, 512).copy(),
        "wd2": asf(inp["L0_ffn_w_down"]).reshape(32 * 512, 1024).copy(),
        "swg": asf(inp["L0_ffn_shared_w_gate"]),
        "swu": asf(inp["L0_ffn_shared_w_up"]),
        "swd": asf(inp["L0_ffn_shared_w_down"]),
        "awg": asf(inp["L0_ffn_anchor_w_gate"]).reshape(16 * 1024, 256).copy(),
        "awu": asf(inp["L0_ffn_anchor_w_up"]).reshape(16 * 1024, 256).copy(),
        "awd": asf(inp["L0_ffn_anchor_w_down"]).reshape(16 * 256, 1024).copy(),
    }
    inB = [{"xb": x1[c], **sharedB} for c in range(8)]
    rB = run_bass_kernel_spmd(ncB, inB, core_ids=list(range(8)))
    x2 = [rB.results[c]["x2"] for c in range(8)]

    ncC1 = bass.Bass("TRN2", target_bir_lowering=False)
    build_launch_C1(ncC1)
    scale = _rope_scale()
    sharedC1 = {
        "wdq": asf(inp["L1_attn_W_DQ"]), "wuq": asf(inp["L1_attn_W_UQ"]),
        "wdkv": asf(inp["L1_attn_W_DKV"]), "wuk": asf(inp["L1_attn_W_UK"]),
        "wuv": asf(inp["L1_attn_W_UV"]), "wqr": asf(inp["L1_attn_W_QR"]),
        "wkr": asf(inp["L1_attn_W_KR"]),
    }
    inC1 = []
    for c in range(8):
        pos0 = (c % 2) * 2048
        rope4 = np.tile(scale[pos0:pos0 + 2048].T, (4, 1)).astype(f32)
        inC1.append({"xc": x2[c], "rope4": rope4, **sharedC1})
    rC1 = run_bass_kernel_spmd(ncC1, inC1, core_ids=list(range(8)))
    qkv = [rC1.results[c]["qkv"] for c in range(8)]

    ncC2 = bass.Bass("TRN2", target_bir_lowering=False)
    build_launch_C2(ncC2)
    inC2 = []
    for c in range(8):
        b, j = c // 2, c % 2
        full = np.concatenate([qkv[2 * b], qkv[2 * b + 1]], axis=1)
        rows = np.concatenate([
            full[j * 512:(j + 1) * 512],
            full[1024 + j * 256:1024 + (j + 1) * 256],
            full[1536 + j * 512:1536 + (j + 1) * 512],
            full[2560:2592],
            full[2592 + j * 512:2592 + (j + 1) * 512]], axis=0)
        inC2.append({"qkvh": np.ascontiguousarray(rows)})
    rC2 = run_bass_kernel_spmd(ncC2, inC2, core_ids=list(range(8)))
    ys = [rC2.results[c]["y"] for c in range(8)]

    ncC3 = bass.Bass("TRN2", target_bir_lowering=False)
    build_launch_C3(ncC3)
    inC3 = []
    for c in range(8):
        b, j = c // 2, c % 2
        yfull = np.concatenate([ys[2 * b], ys[2 * b + 1]], axis=0)
        inC3.append({"xc3": x2[c],
                     "ya": np.ascontiguousarray(yfull[:, j * 2048:(j + 1) * 2048]),
                     "wo": asf(inp["L1_attn_W_O"])})
    rC3 = run_bass_kernel_spmd(ncC3, inC3, core_ids=list(range(8)))
    x3 = [rC3.results[c]["x3"] for c in range(8)]

    ncD0 = bass.Bass("TRN2", target_bir_lowering=False)
    build_launch_D0(ncD0)
    sharedD0 = {
        "nwbc": np.broadcast_to(asf(inp["L1_ffn_attn_nw"]), (128, 1024)).copy(),
        "nbbc": np.broadcast_to(asf(inp["L1_ffn_attn_nb"]), (128, 1024)).copy(),
        "wg": asf(inp["L1_ffn_wg"]),
    }
    inD0 = [{"xd": x3[c], **sharedD0} for c in range(8)]
    rD0 = run_bass_kernel_spmd(ncD0, inD0, core_ids=list(range(8)))
    lnd = np.concatenate([rD0.results[c]["lnd"] for c in range(8)], 0)
    ginfo = np.concatenate([rD0.results[c]["ginfo"] for c in range(8)], 0)
    counts = np.concatenate([rD0.results[c]["counts"] for c in range(8)], 0)

    ncD1 = bass.Bass("TRN2", target_bir_lowering=False)
    build_launch_D1(ncD1)
    iw_all = asf(inp["L1_ffn_inter_w"])
    ib_all = asf(inp["L1_ffn_inter_b"])
    ow_all = asf(inp["L1_ffn_output_w"])
    ob_all = asf(inp["L1_ffn_output_b"])
    inD1 = []
    for e in range(8):
        eoh = np.zeros((128, 8), f32)
        eoh[:, e] = 1
        inD1.append({
            "lndf": lnd, "gall": ginfo, "cnts": counts, "pmask": _pm(e),
            "eohbc": eoh, "iw": iw_all[e].copy(),
            "ib32": ib_all[e].reshape(32, 128).T.copy(),
            "ow": ow_all[e].copy(),
            "obbc": np.broadcast_to(ob_all[e], (128, 1024)).copy(),
        })
    rD1 = run_bass_kernel_spmd(ncD1, inD1, core_ids=list(range(8)))
    eall = np.concatenate([rD1.results[e]["eout"] for e in range(8)], 0)
    eall = np.concatenate([eall, np.zeros((1, 1024), eall.dtype)], 0)

    ncE = bass.Bass("TRN2", target_bir_lowering=False)
    build_launch_E(ncE)
    sharedE = {
        "cnts": counts, "eall": eall,
        "riw": asf(inp["L1_ffn_res_inter_w"]),
        "rib32": asf(inp["L1_ffn_res_inter_b"]).reshape(32, 128).T.copy(),
        "rowt": asf(inp["L1_ffn_res_output_w"]),
        "robc": np.broadcast_to(asf(inp["L1_ffn_res_output_b"]), (128, 1024)).copy(),
        "rcoef": asf(inp["L1_ffn_res_coef"]),
    }
    inE = []
    for c in range(8):
        inE.append({"xe": x3[c], "lnde": lnd[c * 2048:(c + 1) * 2048],
                    "gie": ginfo[c * 2048:(c + 1) * 2048], "pmask": _pm(c),
                    **sharedE})
    rE = run_bass_kernel_spmd(ncE, inE, core_ids=list(range(8)))
    out = np.concatenate([rE.results[c]["x4"] for c in range(8)], 0)
    return out.reshape(4, 4096, 1024)
